# version 10 (final)
# speedup vs baseline: 1.1318x; 1.0031x over previous
import contextlib
import math
import numpy as np

import concourse.bass as bass
import concourse.mybir as mybir
from concourse.bass_utils import run_bass_kernel_spmd

F32 = mybir.dt.float32
BF16 = mybir.dt.bfloat16
AF = mybir.ActivationFunctionType
ALU = mybir.AluOpType
AX = mybir.AxisListType

BSZ = 4
TSEQ = 8192
DM = 1024
DIN = 2048
DCH = 1024
NG = 8
NST = 16
DTR = 64
NXD = 96
DCONV = 4
TH = TSEQ // 2
HALO = 1152
WIN = 512
NH = 8
DH = 64
INNER = 512
HID = 2730
H2 = 2 * HID
QK_SCALE = 8.0
EPS = 1e-5
NEGBIG = -1.0e38

DBG_NO_RS = False
DBG_NO_YDMA = False
DBG_A2_CHUNKS = None
DBG_A2_PARTS = {"dt", "scan", "gate", "out"}
DBG_SP_LEVEL = 9
DBG_TAP = False
TCA = 2048
TCS = 512
TCM = 1024
PAIRS = [[0, 1], [2, 3], [4, 5], [6, 7]]


def _ceil(a, b):
    return -(-a // b)


def host_tables(T0, n=None):
    if n is None:
        n = TSEQ
    half = np.arange(0, DH, 2, dtype=np.float32)
    inv_freq = (1.0 / (10000.0 ** (half / DH))).astype(np.float32)
    t = np.arange(T0 - HALO, T0 + TH, dtype=np.float32)
    freqs = (t[:, None] * inv_freq[None, :]).astype(np.float32)
    freqs = np.concatenate([freqs, freqs], axis=-1)
    base = ((half + 0.4 * DH) / (1.4 * DH)).astype(np.float32)
    power = ((t - n // 2) / float(WIN // 2)).astype(np.float32)
    scale = (base[None, :] ** power[:, None]).astype(np.float32)
    scale = np.concatenate([scale, scale], axis=-1)
    cos = np.cos(freqs).astype(np.float32)
    sin = np.sin(freqs).astype(np.float32)
    sgn = np.concatenate([-np.ones(32, np.float32), np.ones(32, np.float32)])
    qc = (cos * scale) * QK_SCALE
    qs = (sin * scale) * sgn[None, :] * QK_SCALE
    kc = cos / scale
    ks = (sin / scale) * sgn[None, :]
    return (qc.astype(np.float32), qs.astype(np.float32),
            kc.astype(np.float32), ks.astype(np.float32))


def host_masks(is_first_half):
    r = np.arange(128)[:, None]
    j = np.arange(128)[None, :]
    maskLo = np.where(j < r, NEGBIG, 0.0).astype(np.float32)
    maskHi = np.where(j > r, NEGBIG, 0.0).astype(np.float32)
    maskS = np.zeros((8, 128, 640), np.float32)
    if is_first_half:
        maskS[0:4, :, :] = NEGBIG
        for i in range(4):
            qt_start = i * 128
            lim = 512 - qt_start
            maskS[4 + i, :, :lim] = NEGBIG
    return maskLo, maskHi, maskS


def make_core_inputs(c, I):
    b, h = c // 2, c % 2
    D0 = h * DCH
    T0 = h * TH
    d = {}
    x = np.ascontiguousarray(I["L0_attn_seq"][b])
    d["xA"] = x
    xb = np.zeros((HALO + TH, DM), np.float32)
    lo = T0 - HALO
    src_lo = max(lo, 0)
    xb[src_lo - lo:, :] = x[src_lo:T0 + TH]
    d["xB"] = xb
    d["xOwn"] = np.ascontiguousarray(x[T0:T0 + TH])
    d["wInX"] = np.ascontiguousarray(I["L0_attn_in_proj_w"][D0:D0 + DCH, :].T)
    d["wInZ"] = np.ascontiguousarray(I["L0_attn_in_proj_w"][DIN + D0:DIN + D0 + DCH, :].T)
    pg = lambda v: np.ascontiguousarray(v.reshape(NG, 128).T)
    d["convW"] = np.ascontiguousarray(
        I["L0_attn_conv_w"][D0:D0 + DCH, :].reshape(NG, 128, DCONV).transpose(1, 0, 2).reshape(128, NG * DCONV))
    d["convB"] = pg(I["L0_attn_conv_b"][D0:D0 + DCH])
    d["xProjT"] = np.ascontiguousarray(I["L0_attn_x_proj_w"][:, D0:D0 + DCH].T)
    d["dtWT"] = np.ascontiguousarray(I["L0_attn_dt_proj_w"][D0:D0 + DCH, :].T)
    d["dtB"] = pg(I["L0_attn_dt_proj_b"][D0:D0 + DCH])
    d["Alog"] = np.ascontiguousarray(
        I["L0_attn_A_log"][D0:D0 + DCH, :].reshape(NG, 128, NST).transpose(1, 0, 2).reshape(128, NG * NST))
    d["Dvec"] = pg(I["L0_attn_D"][D0:D0 + DCH])
    d["outWT"] = np.ascontiguousarray(I["L0_attn_out_proj_w"][:, D0:D0 + DCH].T)
    d["lnG8"] = np.ascontiguousarray(I["L0_attn_ln_g"].reshape(8, 128).T)
    d["lnB8"] = np.ascontiguousarray(I["L0_attn_ln_b"].reshape(8, 128).T)
    d["rsc"] = np.broadcast_to(I["L0_attn_res_scale"].reshape(1, 1), (128, 1)).copy()
    for l in range(2):
        d[f"wqkv{l}"] = np.ascontiguousarray(I["L0_ffn_wqkv"][l])
        d[f"wo{l}"] = np.ascontiguousarray(I["L0_ffn_wo"][l])
        d[f"w1{l}"] = np.ascontiguousarray(I["L0_ffn_w1"][l])
        d[f"w2{l}"] = np.ascontiguousarray(I["L0_ffn_w2"][l])
        d[f"qsc{l}"] = np.ascontiguousarray(np.broadcast_to(I["L0_ffn_q_scale"][l][None, :], (128, DH)))
        d[f"ksc{l}"] = np.ascontiguousarray(np.broadcast_to(I["L0_ffn_k_scale"][l][None, :], (128, DH)))
        d[f"flnG{l}"] = np.ascontiguousarray(I["L0_ffn_ln_g"][l].reshape(8, 128).T)
        d[f"flnB{l}"] = np.ascontiguousarray(I["L0_ffn_ln_b"][l].reshape(8, 128).T)
        b1 = I["L0_ffn_b1"][l]
        b1a = np.zeros((128, 22), np.float32)
        b1g = np.zeros((128, 22), np.float32)
        for m in range(22):
            w = 128 if m < 21 else HID - 21 * 128
            b1a[:w, m] = b1[m * 128: m * 128 + w]
            b1g[:w, m] = b1[HID + m * 128: HID + m * 128 + w]
        d[f"b1a{l}"] = b1a
        d[f"b1g{l}"] = b1g
        d[f"b2{l}"] = np.ascontiguousarray(I["L0_ffn_b2"][l].reshape(8, 128).T)
    qc, qs, kc, ks = host_tables(T0)
    d["rotQC"], d["rotQS"], d["rotKC"], d["rotKS"] = qc, qs, kc, ks
    mlo, mhi, ms = host_masks(h == 0)
    d["maskLo"], d["maskHi"] = mlo, mhi
    d["maskS"] = np.ascontiguousarray(ms.reshape(8 * 128, 640))
    d["hgate"] = np.full((128, 1), float(h), np.float32)
    d["identF"] = np.eye(128, dtype=np.float32)
    d["ones1"] = np.ones((1, 128), np.float32)
    sel = np.zeros((32, 32, 128), np.float32)
    for r in range(32):
        sel[r, r, :] = 1.0
    d["selBC"] = np.ascontiguousarray(sel.reshape(32, 32 * 128))
    return d


ENGS = ("sync", "vector", "scalar", "tensor", "gpsimd")


class Ch:

    def __init__(self, nc, es, name, step=1):
        self.sem = es.enter_context(nc.semaphore(name))
        self.n = 0
        self.step = step

    def tick(self):
        self.n += self.step
        return self.n


class Rec:

    def __init__(self, mk):
        self.mk = mk
        self.ops = {e: [] for e in ENGS}

    def emit(self, eng, fn, wait=(), inc=()):
        incs = [(ch, ch.tick()) for ch in inc]

        def run(e, fn=fn, wait=tuple(wait), incs=tuple(incs)):
            for ch, v in wait:
                e.wait_ge(ch.sem, v)
            ins = fn(e)
            for ch, _ in incs:
                assert ins is not None
                ins = ins.then_inc(ch.sem, ch.step)
        self.ops[eng].append(run)
        return incs[0][1] if incs else None

    def wait_only(self, eng, wait):
        self.emit(eng, lambda e: None, wait=wait)

    def replay(self, nc):
        with nc.Block() as blk:
            for name in ENGS:
                if not self.ops[name]:
                    continue
                section = getattr(blk, name)

                def body(e, name=name):
                    for fn in self.ops[name]:
                        fn(e)
                section(body)


class MK:

    def __init__(self, nc, es, tseq=TSEQ):
        self.nc = nc
        self.es = es
        self.T = tseq
        self.THALF = tseq // 2
        self.ch = {}
        for name, step in [("dS", 16), ("dA", 16), ("dG", 16),
                           ("cPE", 1), ("cDVE", 1), ("cACT", 1), ("cPOOL", 1),
                           ("cc", 1)]:
            self.ch[name] = Ch(nc, es, "ch_" + name, step)
        self.t = {}
        self.dmach = {}
        self.d = {}
        self.inp = {}

    _uid = 0

    def _uniq(self, name):
        MK._uid += 1
        return f"{name}__{MK._uid}"

    def sb(self, es, name, shape, dtype=F32):
        h = es.enter_context(self.nc.sbuf_tensor(self._uniq(name), list(shape), dtype))
        self.t[name] = h
        return h

    def ps(self, es, name, shape, dtype=F32):
        h = es.enter_context(self.nc.psum_tensor(self._uniq(name), list(shape), dtype))
        self.t[name] = h
        return h

    def dram(self, name, shape, dtype=F32, kind=None):
        if kind:
            h = self.nc.dram_tensor(name, list(shape), dtype, kind=kind)
        else:
            h = self.nc.dram_tensor(name, list(shape), dtype)
        self.d[name] = h
        return h

    def din(self, name, shape, dtype=F32):
        h = self.dram(name, shape, dtype, kind="ExternalInput")
        self.inp[name] = h
        return h

    def _init_track(self):
        self.last_w = {}
        self.readers = {}
        self.seen = {e: {} for e in ENGS}

    def op(self, rec, eng, fn, reads=(), writes=(), tag=None):
        ch = {"sync": self.ch["dS"], "scalar": self.ch["cACT"],
              "vector": self.ch["cDVE"], "tensor": self.ch["cPE"],
              "gpsimd": self.ch["cPOOL"]}[eng]
        waits = {}
        for r in reads:
            w = self.last_w.get(r)
            if w is not None:
                c, v = w
                if c is not ch or True:
                    waits[id(c)] = (c, max(v, waits.get(id(c), (c, 0))[1]))
        for w_ in writes:
            lw = self.last_w.get(w_)
            if lw is not None:
                c, v = lw
                waits[id(c)] = (c, max(v, waits.get(id(c), (c, 0))[1]))
            for c, v in self.readers.get(w_, ()):
                waits[id(c)] = (c, max(v, waits.get(id(c), (c, 0))[1]))
        sn = self.seen[eng]
        wl = [(c, v) for (c, v) in waits.values() if sn.get(id(c), -1) < v]
        for c, v in wl:
            sn[id(c)] = v
        n = rec.emit(eng, fn, wait=wl, inc=[ch])
        for r in reads:
            self.readers.setdefault(r, []).append((ch, n))
        for w_ in writes:
            self.last_w[w_] = (ch, n)
            self.readers[w_] = []
        return n

    def dma(self, rec, q, dst_ap, src_ap, reads=(), writes=(), cast=False, accum=False,
            slow=False, grp=None):
        eng = {"S": "sync", "A": "scalar", "G": "gpsimd"}[q]
        assert writes, "dma needs a write target for its channel"
        import re as _re
        key = grp or _re.sub(r"_\d+$", "", writes[0])
        ch = self.dmach.get(key)
        if ch is None:
            ch = Ch(self.nc, self.es, "dq_" + key, 16)
            self.dmach[key] = ch
        waits = {}
        for r in reads:
            w = self.last_w.get(r)
            if w is not None:
                c, v = w
                waits[id(c)] = (c, max(v, waits.get(id(c), (c, 0))[1]))
        for w_ in writes:
            lw = self.last_w.get(w_)
            if lw is not None:
                c, v = lw
                waits[id(c)] = (c, max(v, waits.get(id(c), (c, 0))[1]))
            for c, v in self.readers.get(w_, ()):
                waits[id(c)] = (c, max(v, waits.get(id(c), (c, 0))[1]))
        if accum:
            def fn(e):
                return e.dma_start(dst_ap, src_ap, accum_op=ALU.add)
        elif slow:
            def fn(e):
                return e.dma_start(dst_ap, src_ap, allow_slow_non_contiguous=True)
        else:
            def fn(e):
                return e.dma_start(dst_ap, src_ap)
        sn = self.seen[eng]
        wl = [(c, v) for (c, v) in waits.values() if sn.get(id(c), -1) < v]
        for c, v in wl:
            sn[id(c)] = v
        n = rec.emit(eng, fn, wait=wl, inc=[ch])
        for r in reads:
            self.readers.setdefault(r, []).append((ch, n))
        for w_ in writes:
            self.last_w[w_] = (ch, n)
            self.readers[w_] = []
        return n

    def group_fix(self, names, grpkey):
        ch = self.dmach[grpkey]
        for nm in names:
            if nm in self.last_w and self.last_w[nm][0] is ch:
                self.last_w[nm] = (ch, ch.n)

    def drain_dmas(self, rec):
        wl = [(ch, ch.n) for ch in self.dmach.values() if ch.n > 0]
        for i in range(0, len(wl), 16):
            rec.wait_only("sync", wl[i:i + 16])

    def declare(self):
        T, THF = self.T, self.THALF
        HT = HALO + THF
        self.din("xA", [T, DM])
        self.din("xB", [HT, DM])
        self.din("xOwn", [THF, DM])
        self.din("wInX", [DM, DCH]); self.din("wInZ", [DM, DCH])
        self.din("convW", [128, NG * DCONV]); self.din("convB", [128, NG])
        self.din("xProjT", [DCH, NXD]); self.din("dtWT", [DTR, DCH])
        self.din("dtB", [128, NG]); self.din("Alog", [128, NG * NST])
        self.din("Dvec", [128, NG]); self.din("outWT", [DCH, DM])
        self.din("lnG8", [128, NG]); self.din("lnB8", [128, NG])
        self.din("rsc", [128, 1]); self.din("hgate", [128, 1])
        for l in range(2):
            self.din(f"wqkv{l}", [DM, 3 * INNER]); self.din(f"wo{l}", [INNER, DM])
            self.din(f"w1{l}", [DM, H2]); self.din(f"w2{l}", [HID, DM])
            self.din(f"qsc{l}", [128, DH]); self.din(f"ksc{l}", [128, DH])
            self.din(f"flnG{l}", [128, NG]); self.din(f"flnB{l}", [128, NG])
            self.din(f"b1a{l}", [128, 22]); self.din(f"b1g{l}", [128, 22])
            self.din(f"b2{l}", [128, NG])
        self.din("rotQC", [HT, DH]); self.din("rotQS", [HT, DH])
        self.din("rotKC", [HT, DH]); self.din("rotKS", [HT, DH])
        self.din("maskLo", [128, 128]); self.din("maskHi", [128, 128])
        self.din("maskS", [8 * 128, 640])
        self.din("identF", [128, 128]); self.din("ones1", [1, 128])
        self.din("selBC", [32, 32 * 128])
        self.dram("OUT", [THF, DM], F32, kind="ExternalOutput")
        kk = "ExternalOutput" if DBG_TAP else None
        self.dram("d_xdblp", [NXD, T]); self.dram("d_xdbl", [NXD, T])
        self.dram("d_xconv", [DCH, T], kind=kk); self.dram("d_sz", [DCH, T], kind=kk)
        self.dram("d_prt", [2 * DM, THF]); self.dram("d_ps", [DM, THF])
        self.dram("d_m0hi", [DM, 1025]); self.dram("d_m0ho", [2 * DM, 1025])
        self.dram("d_xn", [DM, HALO + THF], kind=kk)
        self.dram("d_a", [DM, 1 + THF], kind=kk); self.dram("d_m", [DM, 1 + THF], kind=kk)
        self.dram("d_m1", [DM, 1 + THF], kind=kk); self.dram("d_mh", [DM, 1025])
        if DBG_TAP:
            self.dram("t_xdbl", [NXD, T], kind="ExternalOutput")
            self.dram("t_ps", [DM, THF], kind="ExternalOutput")
        for l in range(2):
            self.dram(f"d_w1b{l}", [DM, H2], BF16)
            self.dram(f"d_w2b{l}", [HID, DM], BF16)
            self.dram(f"d_qkvb{l}", [DM, 3 * INNER], BF16)

    def setup(self, es):
        nc = self.nc
        rec = Rec(self)
        g = self.sb(es, "g_misc", [128, 8 * NG + 64], F32)
        for nm, shape in [("convW", [128, NG * DCONV]), ("convB", [128, NG]),
                          ("dtB", [128, NG]), ("Dvec", [128, NG]),
                          ("lnG8", [128, NG]), ("lnB8", [128, NG]),
                          ("rsc", [128, 1]), ("hgate", [128, 1]),
                          ("maskLo", [128, 128]), ("maskHi", [128, 128]),
                          ("identF", [128, 128]), ("ones1", [1, 128])]:
            t = self.sb(es, "s_" + nm, shape, F32)
            self.dma(rec, "S", t[:, :], self.d[nm].ap(), writes=("s_" + nm,))
        for l in range(2):
            for nm, shape in [(f"qsc{l}", [128, DH]), (f"ksc{l}", [128, DH]),
                              (f"flnG{l}", [128, NG]), (f"flnB{l}", [128, NG]),
                              (f"b1a{l}", [128, 22]), (f"b1g{l}", [128, 22]),
                              (f"b2{l}", [128, NG])]:
                t = self.sb(es, "s_" + nm, shape, F32)
                self.dma(rec, "S", t[:, :], self.d[nm].ap(), writes=("s_" + nm,))
        alog = self.sb(es, "s_alog", [128, NG * NST], F32)
        acol = self.sb(es, "s_acol", [128, NG * NST], F32)
        self.dma(rec, "S", alog[:, :], self.d["Alog"].ap(), writes=("s_alog",))
        self.op(rec, "scalar", lambda e: e.activation(acol[:, :], alog[:, :], AF.Exp),
                reads=("s_alog",), writes=("s_acol",))
        self.op(rec, "vector", lambda e: e.tensor_scalar_mul(acol[:, :], acol[:, :], -1.0),
                reads=("s_acol",), writes=("s_acol",))
        identB = self.sb(es, "s_identB", [128, 128], BF16)
        self.op(rec, "scalar", lambda e: e.copy(identB[:, :], self.t["s_identF"][:, :]),
                reads=("s_identF",), writes=("s_identB",))
        onesC = self.sb(es, "s_onesC", [128, 1], BF16)
        self.op(rec, "vector", lambda e: e.memset(onesC[:, :], 1.0), writes=("s_onesC",))
        epsc = self.sb(es, "s_eps", [128, 1], F32)
        self.op(rec, "vector", lambda e: e.memset(epsc[:, :], EPS), writes=("s_eps",))
        neg1 = self.sb(es, "s_neg1", [128, 1], F32)
        self.op(rec, "vector", lambda e: e.memset(neg1[:, :], -1.0), writes=("s_neg1",))
        e12 = self.sb(es, "s_eps12", [128, 1], F32)
        self.op(rec, "vector", lambda e: e.memset(e12[:, :], 1e-12), writes=("s_eps12",))
        e24 = self.sb(es, "s_eps24", [128, 1], F32)
        self.op(rec, "vector", lambda e: e.memset(e24[:, :], 1e-24), writes=("s_eps24",))
        ngh = self.sb(es, "s_negh", [128, 1], F32)
        self.op(rec, "vector", lambda e: e.memset(ngh[:, :], -0.5), writes=("s_negh",))
        t = self.sb(es, "s_dtWT", [DTR, DCH], F32)
        self.dma(rec, "S", t[:, :], self.d["dtWT"].ap(), writes=("s_dtWT",))
        sel = self.sb(es, "s_selBC", [32, 32 * 128], F32)
        self.dma(rec, "S", sel[:, :], self.d["selBC"].ap(), writes=("s_selBC",))
        xp = self.sb(es, "s_xProjT", [128, NG * NXD], F32)
        for gi in range(NG):
            self.dma(rec, "S", xp[:, gi * NXD:(gi + 1) * NXD],
                     self.d["xProjT"].ap()[gi * 128:(gi + 1) * 128, :], writes=("s_xProjT",))
        ow = self.sb(es, "s_outWTb", [128, NG * DM], BF16)
        for gi in range(NG):
            self.dma(rec, "G", ow[:, gi * DM:(gi + 1) * DM],
                     self.d["outWT"].ap()[gi * 128:(gi + 1) * 128, :], writes=("s_outWTb",))
        self.drain_dmas(rec)
        rec.replay(nc)

    def wprep(self, es):
        rec = Rec(self)
        with contextlib.ExitStack() as les:
            buf = [self.sb(les, f"wp{i}", [128, 1024], BF16) for i in range(4)]
            i = 0
            for l in range(2):
                for nm, dst, rows, cols in [
                    (f"wqkv{l}", f"d_qkvb{l}", DM, 3 * INNER),
                    (f"w1{l}", f"d_w1b{l}", DM, H2),
                    (f"w2{l}", f"d_w2b{l}", HID, DM),
                ]:
                    src = self.d[nm].ap()
                    dd = self.d[dst].ap()
                    for r0 in range(0, rows, 128):
                        rh = min(128, rows - r0)
                        for c0 in range(0, cols, 1024):
                            cw = min(1024, cols - c0)
                            b = buf[i % 4]
                            bn = f"wp{i % 4}"
                            self.dma(rec, "G", b[0:rh, 0:cw], src[r0:r0 + rh, c0:c0 + cw],
                                     writes=(bn,))
                            self.dma(rec, "S", dd[r0:r0 + rh, c0:c0 + cw], b[0:rh, 0:cw],
                                     reads=(bn,), writes=(dst,))
                            i += 1
            self.drain_dmas(rec)
            self.drain_dmas(rec)
        rec.replay(self.nc)

    def ln_stats(self, rec, es_t, xt_name, xt_ap, scr, pfx):
        ns, sq, ssq, t1, var = (scr[pfx + k] for k in ("_ns", "_sq", "_ssq", "_t1", "_var"))
        nsn, sqn, ssqn, t1n, varn = (pfx + k for k in ("_ns", "_sq", "_ssq", "_t1", "_var"))
        self.op(rec, "vector", lambda e: e.tensor_reduce(
            ns[:, :], xt_ap, AX.X, ALU.add, negate=True), reads=(xt_name,), writes=(nsn,))
        self.op(rec, "scalar", lambda e: e.activation(
            sq[:, :], xt_ap, AF.Square, accum_out=ssq[:, :]),
            reads=(xt_name,), writes=(sqn, ssqn))
        self.op(rec, "vector", lambda e: e.tensor_scalar_mul(ns[:, :], ns[:, :], 1.0 / DM),
                reads=(nsn,), writes=(nsn,))
        self.op(rec, "vector", lambda e: e.tensor_mul(t1[:, :], ns[:, :], ns[:, :]),
                reads=(nsn,), writes=(t1n,))
        self.op(rec, "vector", lambda e: e.tensor_scalar(
            var[:, :], ssq[:, :], 1.0 / DM, None, ALU.mult),
            reads=(ssqn,), writes=(varn,))
        self.op(rec, "vector", lambda e: e.tensor_sub(var[:, :], var[:, :], t1[:, :]),
            reads=(varn, t1n), writes=(varn,))
        self.op(rec, "scalar", lambda e: e.activation(var[:, :], var[:, :], AF.Ln,
                                                      bias=self.t["s_eps"][:, 0:1]),
                reads=(varn, "s_eps"), writes=(varn,))
        self.op(rec, "scalar", lambda e: e.activation(var[:, :], var[:, :], AF.Exp,
                                                      scale=self.t["s_negh"][:, 0:1]),
                reads=(varn, "s_negh"), writes=(varn,))
        return ns, var, nsn, varn

    def phase_a1(self, es):
        nc = self.nc
        rec = Rec(self)
        T = self.T
        NCA = T // TCA
        JC = TCA // 128
        with contextlib.ExitStack() as les:
            winxb = self.sb(les, "winxb", [128, NG * DM], BF16)
            winzb = self.sb(les, "winzb", [128, NG * DM], BF16)
            for k in range(NG):
                self.dma(rec, "G", winxb[:, k * DCH:(k + 1) * DCH],
                         self.d["wInX"].ap()[k * 128:(k + 1) * 128, :], writes=("winxb",))
                self.dma(rec, "G", winzb[:, k * DCH:(k + 1) * DCH],
                         self.d["wInZ"].ap()[k * 128:(k + 1) * 128, :], writes=("winzb",))
            sxt = [self.sb(les, f"sxt{i}", [128, DM], F32) for i in range(2)]
            sxn = [self.sb(les, f"sxn{i}", [128, DM], F32) for i in range(2)]
            sxnn = [self.sb(les, f"sxnn{i}", [128, DM], F32) for i in range(2)]
            scr = {}
            for i in range(2):
                for pfx in (f"l1_{i}", f"l2_{i}"):
                    for k, sh in (("_ns", 1), ("_sq", DM), ("_ssq", 1), ("_t1", 1), ("_var", 1)):
                        scr[pfx + k] = self.sb(les, pfx + k, [128, sh], F32)
            psTb = [self.ps(les, f"psTb{i}", [128, 512], F32) for i in range(2)]
            xn2fm = self.sb(les, "xn2fm", [128, NG * TCA], BF16)
            psM = [self.ps(les, f"psM{i}", [128, 512], F32) for i in range(2)]
            xpre = self.sb(les, "xpre", [128, 3 + TCA], F32)
            xtail = self.sb(les, "xtail", [128, NG * 3], F32)
            xc = [self.sb(les, f"xc{i}", [128, TCA], F32) for i in range(2)]
            xcv = self.sb(les, "xcv", [128, TCA], F32)
            szv = self.sb(les, "szv", [128, TCA], F32)
            psXD = [self.ps(les, f"psXD{i}", [96, 512], F32) for i in range(4)]
            xdblp = self.sb(les, "xdblp", [96, TCA], F32)
            self.op(rec, "vector", lambda e: e.memset(xtail[:, :], 0.0), writes=("xtail",))

            sg8 = self.t["s_lnG8"]
            sb8 = self.t["s_lnB8"]
            for c in range(NCA):
                for j in range(JC):
                    jj = c * JC + j
                    p = jj % 2
                    xt, xn, xnn = sxt[p], sxn[p], sxnn[p]
                    xtn, xnn_n = f"sxt{p}", f"sxnn{p}"
                    xnname = f"sxn{p}"
                    self.dma(rec, "S", xt[:, :], self.d["xA"].ap()[jj * 128:(jj + 1) * 128, :],
                             writes=(xtn,))
                    negmu, rstd, nmn, rsn = self.ln_stats(rec, les, xtn, xt[:, :], scr, f"l1_{p}")
                    self.op(rec, "vector", lambda e, xt=xt, xn=xn, negmu=negmu, rstd=rstd:
                            e.tensor_scalar(xn[:, :], xt[:, :], negmu[:, :], rstd[:, :],
                                            ALU.add, ALU.mult),
                            reads=(xtn, nmn, rsn), writes=(xnname,))
                    negmu2, rstd2, nmn2, rsn2 = self.ln_stats(rec, les, xnname, xn[:, :], scr, f"l2_{p}")
                    self.op(rec, "vector", lambda e, xn=xn, xnn=xnn, negmu2=negmu2, rstd2=rstd2:
                            e.tensor_scalar(xnn[:, :], xn[:, :], negmu2[:, :], rstd2[:, :],
                                            ALU.add, ALU.mult),
                            reads=(xnname, nmn2, rsn2), writes=(xnn_n,))
                    for half in range(2):
                        pt = psTb[half]
                        ptn = f"psTb{half}"
                        for q in range(4):
                            k = half * 4 + q
                            self.op(rec, "tensor", lambda e, pt=pt, q=q, xnn=xnn, k=k:
                                    e.transpose(pt[:, q * 128:(q + 1) * 128],
                                                xnn[:, k * 128:(k + 1) * 128],
                                                self.t["s_identF"][:, :]),
                                    reads=(xnn_n, "s_identF"), writes=(ptn,))
                        for q in range(4):
                            k = half * 4 + q
                            self.op(rec, "scalar", lambda e, pt=pt, q=q, k=k, j=j:
                                    e.activation(xn2fm[:, k * TCA + j * 128: k * TCA + (j + 1) * 128],
                                                 pt[:, q * 128:(q + 1) * 128], AF.Identity,
                                                 scale=sg8[:, k:k + 1], bias=sb8[:, k:k + 1]),
                                    reads=(ptn, "s_lnG8", "s_lnB8"), writes=("xn2fm",))
                for gi in range(NG):
                    for mc in range(TCA // 512):
                        pm = psM[mc % 2]
                        pmn = f"psM{mc % 2}"
                        for k in range(NG):
                            self.op(rec, "tensor", lambda e, pm=pm, k=k, gi=gi, mc=mc:
                                    e.matmul(pm[:, :],
                                             winxb[:, k * DCH + gi * 128: k * DCH + (gi + 1) * 128],
                                             xn2fm[:, k * TCA + mc * 512: k * TCA + mc * 512 + 512],
                                             start=(k == 0), stop=(k == NG - 1)),
                                    reads=("winxb", "xn2fm"), writes=(pmn,))
                        self.op(rec, "vector", lambda e, pm=pm, mc=mc:
                                e.tensor_copy(xpre[:, 3 + mc * 512: 3 + mc * 512 + 512], pm[:, :]),
                                reads=(pmn,), writes=("xpre",))
                    cw = self.t["s_convW"]
                    cb = self.t["s_convB"]
                    self.op(rec, "vector", lambda e, gi=gi:
                            e.tensor_copy(xpre[:, 0:3], xtail[:, gi * 3:gi * 3 + 3]),
                            reads=("xtail",), writes=("xpre",))
                    self.op(rec, "vector", lambda e, gi=gi:
                            e.tensor_scalar(xc[0][:, :], bass.AP(xpre, 0, [[3 + TCA, 128], [1, TCA]]),
                                            cw[:, gi * DCONV:gi * DCONV + 1],
                                            cb[:, gi:gi + 1], ALU.mult, ALU.add),
                            reads=("xpre", "s_convW", "s_convB"), writes=("xc0",))
                    for kk in (1, 2, 3):
                        src, dst = xc[(kk + 1) % 2], xc[kk % 2]
                        sn, dn = f"xc{(kk + 1) % 2}", f"xc{kk % 2}"
                        self.op(rec, "vector", lambda e, kk=kk, src=src, dst=dst, gi=gi:
                                e.scalar_tensor_tensor(dst[:, :],
                                                       bass.AP(xpre, kk, [[3 + TCA, 128], [1, TCA]]),
                                                       cw[:, gi * DCONV + kk:gi * DCONV + kk + 1],
                                                       src[:, :], ALU.mult, ALU.add),
                                reads=("xpre", "s_convW", sn), writes=(dn,))
                    self.op(rec, "scalar", lambda e: e.activation(xcv[:, :], xc[1][:, :], AF.Silu),
                            reads=("xc1",), writes=("xcv",))
                    self.op(rec, "vector", lambda e, gi=gi:
                            e.tensor_copy(xtail[:, gi * 3:gi * 3 + 3], xpre[:, TCA:TCA + 3]),
                            reads=("xpre",), writes=("xtail",))
                    self.dma(rec, "S", self.d["d_xconv"].ap()[gi * 128:(gi + 1) * 128,
                                                              c * TCA:(c + 1) * TCA],
                             xcv[:, :], reads=("xcv",), writes=("d_xconv",))
                    xpt = self.t["s_xProjT"]
                    for mc in range(TCA // 512):
                        self.op(rec, "tensor", lambda e, mc=mc, gi=gi:
                                e.matmul(psXD[mc][:, :], xpt[:, gi * NXD:(gi + 1) * NXD],
                                         xcv[:, mc * 512:mc * 512 + 512],
                                         start=(gi == 0), stop=(gi == NG - 1)),
                                reads=("s_xProjT", "xcv"), writes=(f"psXD{mc}",))
                for gi in range(NG):
                    for mc in range(TCA // 512):
                        pm = psM[mc % 2]
                        pmn = f"psM{mc % 2}"
                        for k in range(NG):
                            self.op(rec, "tensor", lambda e, pm=pm, k=k, gi=gi, mc=mc:
                                    e.matmul(pm[:, :],
                                             winzb[:, k * DCH + gi * 128: k * DCH + (gi + 1) * 128],
                                             xn2fm[:, k * TCA + mc * 512: k * TCA + mc * 512 + 512],
                                             start=(k == 0), stop=(k == NG - 1)),
                                    reads=("winzb", "xn2fm"), writes=(pmn,))
                        self.op(rec, "scalar", lambda e, pm=pm, mc=mc:
                                e.activation(szv[:, mc * 512:mc * 512 + 512], pm[:, :], AF.Silu),
                                reads=(pmn,), writes=("szv",))
                    self.dma(rec, "S", self.d["d_sz"].ap()[gi * 128:(gi + 1) * 128,
                                                           c * TCA:(c + 1) * TCA],
                             szv[:, :], reads=("szv",), writes=("d_sz",))
                for mc in range(TCA // 512):
                    self.op(rec, "vector", lambda e, mc=mc:
                            e.tensor_copy(xdblp[:, mc * 512:mc * 512 + 512], psXD[mc][:, :]),
                            reads=(f"psXD{mc}",), writes=("xdblp",))
                self.dma(rec, "S", self.d["d_xdblp"].ap()[:, c * TCA:(c + 1) * TCA],
                         xdblp[:, :], reads=("xdblp",), writes=("d_xdblp",))
            self.op(rec, "gpsimd", lambda e: e.collective_compute(
                "AllReduce", ALU.add, replica_groups=PAIRS,
                ins=[self.d["d_xdblp"].ap().opt()], outs=[self.d["d_xdbl"].ap().opt()]),
                reads=("d_xdblp",), writes=("d_xdbl",))
            if DBG_TAP:
                self.dma(rec, "S", self.d["t_xdbl"].ap()[:, :], self.d["d_xdbl"].ap()[:, :],
                         reads=("d_xdbl",), writes=("t_xdbl",))
            self.drain_dmas(rec)
            self.drain_dmas(rec)
        rec.replay(nc)

    def phase_a2(self, es):
        nc = self.nc
        rec = Rec(self)
        T = self.T
        NCS = T // TCS
        acol = self.t["s_acol"]
        dtw = self.t["s_dtWT"]
        with contextlib.ExitStack() as les:
            sxd = self.sb(les, "sxd", [NXD, TCS], F32)
            sBC = self.sb(les, "sBC", [32, TCS], F32)
            psBC = [self.ps(les, f"psBC{i}", [128, TCS], F32) for i in range(2)]
            bbc = [self.sb(les, f"bbc{i}", [128, TCS], F32) for i in range(2)]
            cbc = [self.sb(les, f"cbc{i}", [128, TCS], F32) for i in range(2)]
            psA2 = self.ps(les, "psA2", [128, TCS], F32)
            dt_all = self.sb(les, "dt_all", [128, NG * TCS], F32)
            u_all = self.sb(les, "u_all", [128, NG * TCS], F32)
            xcva = [self.sb(les, f"xcva{i}", [128, TCS], F32) for i in range(2)]
            szva = [self.sb(les, f"szva{i}", [128, TCS], F32) for i in range(2)]
            y_all = self.sb(les, "y_all", [128, NG * TCS], F32)
            y_b = self.sb(les, "y_b", [128, NG * TCS], F32)
            tmp_all = [self.sb(les, f"tmp_all{i}", [128, NG * TCS], F32) for i in range(2)]
            sp1 = self.sb(les, "sp1", [128, TCS], F32)
            sp2 = self.sb(les, "sp2", [128, TCS], F32)
            dA = [self.sb(les, f"dA{i}", [128, TCS], F32) for i in range(2)]
            uB = [self.sb(les, f"uB{i}", [128, TCS], F32) for i in range(2)]
            hsc = [self.sb(les, f"hsc{i}", [128, TCS], F32) for i in range(2)]
            hst = [self.sb(les, f"hst{g}", [128, NST], F32) for g in range(NG)]
            ygb = self.sb(les, "ygb", [128, NG * TCS], BF16)
            psO = [self.ps(les, f"psO{i}", [128, TCS], F32) for i in range(2)]
            oprt = self.sb(les, "oprt", [128, TCS], F32)
            for g in range(NG):
                self.op(rec, "vector", lambda e, g=g: e.memset(hst[g][:, :], 0.0),
                        writes=(f"hst{g}",))
            dtb = self.t["s_dtB"]
            dvec = self.t["s_Dvec"]
            rsc = self.t["s_rsc"]
            ones1 = self.t["s_ones1"]
            ow = self.t["s_outWTb"]
            for c in range(NCS if DBG_A2_CHUNKS is None else DBG_A2_CHUNKS):
                c0 = c * TCS
                self.dma(rec, "S", sxd[:, :], self.d["d_xdbl"].ap()[:, c0:c0 + TCS],
                         reads=("d_xdbl",), writes=("sxd",))
                self.op(rec, "vector", lambda e: e.memset(y_b[:, 0:NG * TCS // 2], 0.0),
                        writes=("yBlo",))
                self.op(rec, "vector", lambda e: e.memset(y_b[:, NG * TCS // 2:], 0.0),
                        writes=("yBhi",))
                for gi in range(NG if "dt" in DBG_A2_PARTS else 0):
                    self.op(rec, "tensor", lambda e, gi=gi:
                            e.matmul(psA2[:, :], dtw[0:DTR, gi * 128:(gi + 1) * 128],
                                     sxd[0:DTR, :], start=True, stop=True),
                            reads=("s_dtWT", "sxd"), writes=("psA2",))
                    if DBG_SP_LEVEL >= 1:
                        self.op(rec, "scalar", lambda e, gi=gi:
                                e.activation(sp1[:, :], psA2[:, :], AF.Abs, bias=dtb[:, gi:gi + 1]),
                                reads=("psA2", "s_dtB"), writes=("sp1",))
                    else:
                        self.op(rec, "scalar", lambda e: e.copy(sp1[:, :], psA2[:, :]),
                                reads=("psA2",), writes=("sp1",))
                    if DBG_SP_LEVEL >= 2:
                        self.op(rec, "scalar", lambda e:
                                e.activation(sp1[:, :], sp1[:, :], AF.Exp,
                                             scale=self.t["s_neg1"][:, 0:1]),
                                reads=("sp1", "s_neg1"), writes=("sp1",))
                    if DBG_SP_LEVEL >= 3:
                        self.op(rec, "scalar", lambda e:
                                e.activation(sp1[:, :], sp1[:, :], AF.Ln, bias=1.0),
                                reads=("sp1",), writes=("sp1",))
                    if DBG_SP_LEVEL >= 4:
                        self.op(rec, "scalar", lambda e, gi=gi:
                                e.activation(sp2[:, :], psA2[:, :], AF.Relu,
                                             bias=dtb[:, gi:gi + 1]),
                                reads=("psA2", "s_dtB"), writes=("sp2",))
                        self.op(rec, "vector", lambda e, gi=gi:
                                e.tensor_add(dt_all[:, gi * TCS:(gi + 1) * TCS], sp1[:, :], sp2[:, :]),
                                reads=("sp1", "sp2"), writes=(f"dt_all{gi}",))
                    else:
                        self.op(rec, "vector", lambda e, gi=gi:
                                e.tensor_copy(dt_all[:, gi * TCS:(gi + 1) * TCS], sp1[:, :]),
                                reads=("sp1",), writes=(f"dt_all{gi}",))
                    gp = gi % 2
                    self.dma(rec, "A", xcva[gp][:, :],
                             self.d["d_xconv"].ap()[gi * 128:(gi + 1) * 128, c0:c0 + TCS],
                             reads=("d_xconv",), writes=(f"xcva{gp}",))
                    self.op(rec, "vector", lambda e, gi=gi, gp=gp:
                            e.tensor_mul(u_all[:, gi * TCS:(gi + 1) * TCS],
                                         dt_all[:, gi * TCS:(gi + 1) * TCS],
                                         xcva[gp][:, :]),
                            reads=(f"dt_all{gi}", f"xcva{gp}"), writes=(f"u_all{gi}",))
                    self.op(rec, "vector", lambda e, gi=gi, gp=gp:
                            e.tensor_scalar_mul(y_all[:, gi * TCS:(gi + 1) * TCS],
                                                xcva[gp][:, :],
                                                dvec[:, gi:gi + 1]),
                            reads=(f"xcva{gp}", "s_Dvec"),
                            writes=("yAlo" if gi < 4 else "yAhi",))
                self.dma(rec, "A", sBC[:, :],
                         self.d["d_xdbl"].ap()[DTR:DTR + 32, c0:c0 + TCS],
                         reads=("d_xdbl",), writes=("sBC",))
                selt = self.t["s_selBC"]
                for s in range(NST if "scan" in DBG_A2_PARTS else 0):
                    sp = s % 2
                    self.op(rec, "tensor", lambda e, s=s: e.matmul(
                        psBC[0][:, :], selt[:, s * 128:(s + 1) * 128], sBC[:, :],
                        start=True, stop=True),
                        reads=("s_selBC", "sBC"), writes=("psBC0",))
                    self.op(rec, "tensor", lambda e, s=s: e.matmul(
                        psBC[1][:, :], selt[:, (NST + s) * 128:(NST + s + 1) * 128], sBC[:, :],
                        start=True, stop=True),
                        reads=("s_selBC", "sBC"), writes=("psBC1",))
                    self.op(rec, "vector", lambda e, sp=sp: e.tensor_copy(bbc[sp][:, :], psBC[0][:, :]),
                            reads=("psBC0",), writes=(f"bbc{sp}",))
                    self.op(rec, "scalar", lambda e, sp=sp: e.copy(cbc[sp][:, :], psBC[1][:, :]),
                            reads=("psBC1",), writes=(f"cbc{sp}",))
                    for gi in range(NG):
                        gs = gi * NST + s
                        p = gi % 2
                        dAp, uBp, hscp = dA[p], uB[p], hsc[p]
                        dan, ubn, hn = f"dA{p}", f"uB{p}", f"hsc{p}"
                        self.op(rec, "scalar", lambda e, gi=gi, gs=gs, dAp=dAp:
                                e.activation(dAp[:, :], dt_all[:, gi * TCS:(gi + 1) * TCS],
                                             AF.Exp, scale=acol[:, gs:gs + 1]),
                                reads=(f"dt_all{gi}", "s_acol"), writes=(dan,))
                        self.op(rec, "vector", lambda e, gi=gi, uBp=uBp, sp=sp:
                                e.tensor_mul(uBp[:, :], u_all[:, gi * TCS:(gi + 1) * TCS],
                                             bbc[sp][:, :]),
                                reads=(f"u_all{gi}", f"bbc{sp}"), writes=(ubn,))
                        self.op(rec, "vector", lambda e, gi=gi, s=s, dAp=dAp, uBp=uBp, hscp=hscp:
                                e.tensor_tensor_scan(hscp[:, :], dAp[:, :], uBp[:, :],
                                                     hst[gi][:, s:s + 1], ALU.mult, ALU.add),
                                reads=(dan, ubn, f"hst{gi}"), writes=(hn,))
                        self.op(rec, "vector", lambda e, gi=gi, s=s, hscp=hscp:
                                e.tensor_copy(hst[gi][:, s:s + 1], hscp[:, TCS - 1:TCS]),
                                reads=(hn,), writes=(f"hst{gi}",))
                        teng = "vector"
                        self.op(rec, teng, lambda e, gi=gi, hscp=hscp, sp=sp:
                                e.tensor_mul(tmp_all[sp][:, gi * TCS:(gi + 1) * TCS],
                                             hscp[:, :], cbc[sp][:, :]),
                                reads=(hn, f"cbc{sp}"), writes=(f"tmp{sp}_{gi}",))
                    half = NG * TCS // 2
                    ydst, ydn = (y_all, "yA") if sp == 0 else (y_b, "yB")
                    self.dma(rec, "G", ydst[:, 0:half], tmp_all[sp][:, 0:half],
                             reads=tuple(f"tmp{sp}_{g}" for g in range(4)),
                             writes=(ydn + "lo",), grp=ydn + "lo", accum=True)
                    self.dma(rec, "G", ydst[:, half:], tmp_all[sp][:, half:],
                             reads=tuple(f"tmp{sp}_{g}" for g in range(4, 8)),
                             writes=(ydn + "hi",), grp=ydn + "hi", accum=True)
                for gi in range(NG if "gate" in DBG_A2_PARTS else 0):
                    yh = "lo" if gi < 4 else "hi"
                    self.op(rec, "vector", lambda e, gi=gi:
                            e.tensor_add(y_all[:, gi * TCS:(gi + 1) * TCS],
                                         y_all[:, gi * TCS:(gi + 1) * TCS],
                                         y_b[:, gi * TCS:(gi + 1) * TCS]),
                            reads=("yA" + yh, "yB" + yh), writes=("yA" + yh,))
                    gp = gi % 2
                    self.dma(rec, "A", szva[gp][:, :],
                             self.d["d_sz"].ap()[gi * 128:(gi + 1) * 128, c0:c0 + TCS],
                             reads=("d_sz",), writes=(f"szva{gp}",))
                    self.op(rec, "vector", lambda e, gi=gi, gp=gp:
                            e.scalar_tensor_tensor(ygb[:, gi * TCS:(gi + 1) * TCS],
                                                   y_all[:, gi * TCS:(gi + 1) * TCS],
                                                   rsc[:, 0:1],
                                                   szva[gp][:, :],
                                                   ALU.mult, ALU.mult),
                            reads=("yA" + ("lo" if gi < 4 else "hi"), "s_rsc", f"szva{gp}"),
                            writes=("ygb",))
                for m in range(NG if "out" in DBG_A2_PARTS else 0):
                    po = psO[m % 2]
                    pon = f"psO{m % 2}"
                    for gi in range(NG):
                        self.op(rec, "tensor", lambda e, po=po, m=m, gi=gi:
                                e.matmul(po[:, :],
                                         ow[:, gi * DM + m * 128: gi * DM + (m + 1) * 128],
                                         ygb[:, gi * TCS:(gi + 1) * TCS],
                                         start=(gi == 0), stop=(gi == NG - 1)),
                                reads=("s_outWTb", "ygb"), writes=(pon,))
                    self.op(rec, "vector", lambda e, po=po:
                            e.tensor_copy(oprt[:, :], po[:, :]),
                            reads=(pon,), writes=("oprt",))
                    half = c0 // self.THALF
                    lc = c0 % self.THALF
                    self.dma(rec, "S",
                             self.d["d_prt"].ap()[half * DM + m * 128: half * DM + (m + 1) * 128,
                                                  lc:lc + TCS],
                             oprt[:, :], reads=("oprt",), writes=("d_prt",))
            if not DBG_NO_RS:
                self.op(rec, "gpsimd", lambda e: e.collective_compute(
                    "ReduceScatter", ALU.add, replica_groups=PAIRS,
                    ins=[self.d["d_prt"].ap().opt()], outs=[self.d["d_ps"].ap().opt()]),
                    reads=("d_prt",), writes=("d_ps",))
                if DBG_TAP:
                    self.dma(rec, "S", self.d["t_ps"].ap()[:, :], self.d["d_ps"].ap()[:, :],
                             reads=("d_ps",), writes=("t_ps",))
            self.drain_dmas(rec)
            self.drain_dmas(rec)
        rec.replay(nc)

    def phase_bprep(self, es):
        rec = Rec(self)
        WXN = HALO + self.THALF
        NJ = WXN // 128
        with contextlib.ExitStack() as les:
            sxt = [self.sb(les, f"bxt{i}", [128, DM], F32) for i in range(2)]
            sxn = [self.sb(les, f"bxn{i}", [128, DM], F32) for i in range(2)]
            scr = {}
            for i in range(2):
                pfx = f"bl_{i}"
                for k, sh in (("_ns", 1), ("_sq", DM), ("_ssq", 1), ("_t1", 1), ("_var", 1)):
                    scr[pfx + k] = self.sb(les, pfx + k, [128, sh], F32)
            psTf = [self.ps(les, f"psTf{i}", [128, 512], F32) for i in range(2)]
            stg = self.sb(les, "stg", [128, NG * 512], F32)
            for j in range(NJ):
                p = j % 2
                xt, xn = sxt[p], sxn[p]
                xtn, xnn = f"bxt{p}", f"bxn{p}"
                self.dma(rec, "S", xt[:, :], self.d["xB"].ap()[j * 128:(j + 1) * 128, :],
                         writes=(xtn,))
                negmu, rstd, nmn, rsn = self.ln_stats(rec, les, xtn, xt[:, :], scr, f"bl_{p}")
                self.op(rec, "vector", lambda e, xt=xt, xn=xn, negmu=negmu, rstd=rstd:
                        e.tensor_scalar(xn[:, :], xt[:, :], negmu[:, :], rstd[:, :],
                                        ALU.add, ALU.mult),
                        reads=(xtn, nmn, rsn), writes=(xnn,))
                jq = j % 4
                for half in range(2):
                    pt = psTf[half]
                    ptn = f"psTf{half}"
                    for q in range(4):
                        k = half * 4 + q
                        self.op(rec, "tensor", lambda e, pt=pt, q=q, xn=xn, k=k:
                                e.transpose(pt[:, q * 128:(q + 1) * 128],
                                            xn[:, k * 128:(k + 1) * 128],
                                            self.t["s_identF"][:, :]),
                                reads=(xnn, "s_identF"), writes=(ptn,))
                    for q in range(4):
                        k = half * 4 + q
                        self.op(rec, "vector", lambda e, pt=pt, q=q, k=k, jq=jq:
                                e.tensor_copy(stg[:, k * 512 + jq * 128: k * 512 + (jq + 1) * 128],
                                              pt[:, q * 128:(q + 1) * 128]),
                                reads=(ptn,), writes=("stg",))
                if jq == 3 or j == NJ - 1:
                    w = (jq + 1) * 128
                    c0 = (j - jq) * 128
                    for k in range(NG):
                        self.dma(rec, "S",
                                 self.d["d_xn"].ap()[k * 128:(k + 1) * 128, c0:c0 + w],
                                 stg[:, k * 512: k * 512 + w], reads=("stg",), writes=("d_xn",))
            self.drain_dmas(rec)
            self.drain_dmas(rec)
        rec.replay(self.nc)

    def phase_attn(self, es, l):
        rec = Rec(self)
        nc = self.nc
        NB = self.THALF // WIN
        with contextlib.ExitStack() as les:
            wqkvb = self.sb(les, "wqkvb", [128, NG * 3 * INNER], BF16)
            for k in range(NG):
                self.dma(rec, "S", wqkvb[:, k * 3 * INNER:(k + 1) * 3 * INNER],
                         self.d[f"d_qkvb{l}"].ap()[k * 128:(k + 1) * 128, :],
                         reads=(f"d_qkvb{l}",), writes=("wqkvb",))
            wob = self.sb(les, "wob", [128, 4 * DM], BF16)
            for k in range(4):
                self.dma(rec, "G", wob[:, k * DM:(k + 1) * DM],
                         self.d[f"wo{l}"].ap()[k * 128:(k + 1) * 128, :], writes=("wob",))
            xsl = [[self.sb(les, f"xsl{r}_{k}", [128, 513], F32) for k in range(NG)]
                   for r in range(3)]
            xslb = [[self.sb(les, f"xslb{r}_{k}", [128, 513], BF16) for k in range(NG)]
                    for r in range(3)]
            kfm = [self.sb(les, f"kfm{r}", [128, 4 * 512], BF16) for r in range(3)]
            vtm = [self.sb(les, f"vtm{r}", [128, 4 * 512], BF16) for r in range(3)]
            psQK = [self.ps(les, f"psQK{i}", [128, 512], F32) for i in range(2)]
            psSim = [self.ps(les, f"psSim{i}", [128, 640], F32) for i in range(2)]
            psTb6 = self.ps(les, "psTb6", [128, 640], BF16)
            psAV = self.ps(les, "psAV", [128, 128], F32)
            sqscr = self.sb(les, "sqscr", [128, 512], F32)
            nrm = self.sb(les, "nrm", [128, 8], F32)
            knf = self.sb(les, "knf", [128, 512], F32)
            rt1 = self.sb(les, "rt1", [128, 512], F32)
            rt2 = self.sb(les, "rt2", [128, 512], F32)
            rtc = [self.sb(les, f"rtc{i}", [128, 64], F32) for i in range(2)]
            rts = [self.sb(les, f"rts{i}", [128, 64], F32) for i in range(2)]
            tabc = [self.sb(les, f"tabc{i}", [128, 64], F32) for i in range(2)]
            tabs_ = [self.sb(les, f"tabs{i}", [128, 64], F32) for i in range(2)]
            qtb = self.sb(les, "qtb", [128, 512], BF16)
            qfm = self.sb(les, "qfm", [128, 4 * 512], BF16)
            attn = [self.sb(les, f"attn{i}", [128, 640], BF16) for i in range(2)]
            attnf = [self.sb(les, "attnf0", [128, 640], F32)] * 2
            attnT = [self.sb(les, f"attnT{i}", [128, 640], BF16) for i in range(2)]
            nmax = [self.sb(les, f"nmax{i}", [128, 1], F32) for i in range(2)]
            den = [self.sb(les, f"den{i}", [128, 1], F32) for i in range(2)]
            avfm = self.sb(les, "avfm", [128, 4 * 512], BF16)
            msk = self.sb(les, "msk", [128, 640], F32)
            aout = [self.sb(les, f"aout{i}", [128, 512], F32) for i in range(2)]
            hg = self.t["s_hgate"]
            qsc = self.t[f"s_qsc{l}"]
            ksc = self.t[f"s_ksc{l}"]

            def slab_src(kb):
                t0 = kb * WIN
                if l == 0:
                    return self.d["d_xn"], "d_xn", HALO + t0 - 1
                if kb >= 0:
                    return self.d["d_m"], "d_m", t0
                return self.d["d_mh"], "d_mh", t0 + 1024
            for kb in range(-2, NB):
                r = (kb + 2) % 3
                dsrc, dname, c0 = slab_src(kb)
                for k in range(NG):
                    self.dma(rec, "S", xsl[r][k][:, :],
                             dsrc.ap()[k * 128:(k + 1) * 128, c0:c0 + 513],
                             reads=(dname,), writes=(f"xsl{r}_{k}",), grp=f"xsl{r}")
                self.group_fix([f"xsl{r}_{k}" for k in range(NG)], f"xsl{r}")
                for k in range(NG):
                    self.op(rec, "scalar", lambda e, k=k, r=r:
                            e.copy(xslb[r][k][:, :], xsl[r][k][:, :]),
                            reads=(f"xsl{r}_{k}",), writes=(f"xslb{r}_{k}",))
                is_q = kb >= -1
                for j in range(4):
                    for which, ps, psn, cols in (("k", psQK[0], "psQK0", (512, 1024)),
                                                 ("v", psQK[1], "psQK1", (1024, 1536)),
                                                 ("q", psQK[0], "psQK0", (0, 512))):
                        if which == "q" and not is_q:
                            continue
                        for k in range(NG):
                            off = 1 if k < 4 else 0
                            self.op(rec, "tensor", lambda e, k=k, j=j, off=off, r=r, ps=ps, cols=cols:
                                    e.matmul(ps[:, :],
                                             xslb[r][k][:, off + j * 128: off + j * 128 + 128],
                                             wqkvb[:, k * 3 * INNER + cols[0]: k * 3 * INNER + cols[1]],
                                             start=(k == 0), stop=(k == NG - 1)),
                                    reads=(f"xslb{r}_{k}", "wqkvb"), writes=(psn,))
                        if which == "v":
                            self.op(rec, "vector", lambda e, j=j, r=r, ps=ps:
                                    e.tensor_copy(vtm[r][:, j * 512:(j + 1) * 512], ps[:, :]),
                                    reads=(psn,), writes=(f"vtm{r}",))
                            continue
                        psX, psXn = ps, psn
                        self.op(rec, "scalar", lambda e, psX=psX:
                                e.activation(sqscr[:, :], psX[:, :], AF.Square),
                                reads=(psXn,), writes=("sqscr",))
                        self.op(rec, "vector", lambda e:
                                e.tensor_reduce(nrm[:, :],
                                                bass.AP(sqscr, 0, [[512, 128], [64, 8], [1, 64]]),
                                                AX.X, ALU.add),
                                reads=("sqscr",), writes=("nrm",))
                        self.op(rec, "scalar", lambda e:
                                e.activation(nrm[:, :], nrm[:, :], AF.Ln,
                                             bias=self.t["s_eps24"][:, 0:1]),
                                reads=("nrm", "s_eps24"), writes=("nrm",))
                        self.op(rec, "scalar", lambda e:
                                e.activation(nrm[:, :], nrm[:, :], AF.Exp,
                                             scale=self.t["s_negh"][:, 0:1]),
                                reads=("nrm", "s_negh"), writes=("nrm",))
                        self.op(rec, "vector", lambda e, psX=psX:
                                e.tensor_mul(knf[:, :], psX[:, :],
                                             bass.AP(nrm, 0, [[8, 128], [1, 8], [0, 64]])),
                                reads=(psXn, "nrm"), writes=("knf",))
                        row0 = HALO + kb * WIN + j * 128
                        tc_, ts_ = (("rotKC", "rotKS") if which == "k" else ("rotQC", "rotQS"))
                        sc = ksc if which == "k" else qsc
                        scn = f"s_ksc{l}" if which == "k" else f"s_qsc{l}"
                        jp = j % 2
                        rtcp, rtsp = rtc[jp], rts[jp]
                        self.dma(rec, "A", rtcp[:, :], self.d[tc_].ap()[row0:row0 + 128, :],
                                 writes=(f"rtc{jp}",))
                        self.dma(rec, "A", rtsp[:, :], self.d[ts_].ap()[row0:row0 + 128, :],
                                 writes=(f"rts{jp}",))
                        self.op(rec, "vector", lambda e, sc=sc:
                                e.tensor_mul(knf[:, :], knf[:, :],
                                             bass.AP(sc, 0, [[DH, 128], [0, 8], [1, DH]])),
                                reads=("knf", scn), writes=("knf",))
                        self.op(rec, "vector", lambda e, rtcp=rtcp:
                                e.tensor_mul(rt1[:, :], knf[:, :],
                                             bass.AP(rtcp, 0, [[64, 128], [0, 8], [1, 64]])),
                                reads=("knf", f"rtc{jp}"), writes=("rt1",))
                        self.op(rec, "vector", lambda e, rtsp=rtsp:
                                e.tensor_mul(rt2[:, :],
                                             bass.AP(knf, 32, [[512, 128], [64, 8], [-32, 2], [1, 32]]),
                                             bass.AP(rtsp, 0, [[64, 128], [0, 8], [1, 64]])),
                                reads=("knf", f"rts{jp}"), writes=("rt2",))
                        dstt, dsn = (kfm[r], f"kfm{r}") if which == "k" else (qfm, "qfm")
                        self.op(rec, "vector", lambda e:
                                e.tensor_add(qtb[:, :], rt1[:, :], rt2[:, :]),
                                reads=("rt1", "rt2"), writes=("qtb",))
                        for q in range(4):
                            self.op(rec, "tensor", lambda e, q=q:
                                    e.transpose(psTb6[:, q * 128:(q + 1) * 128],
                                                qtb[:, q * 128:(q + 1) * 128],
                                                self.t["s_identB"][:, :]),
                                    reads=("qtb", "s_identB"), writes=("psTb6",))
                        for q in range(4):
                            self.op(rec, "vector", lambda e, q=q, dstt=dstt, j=j:
                                    e.tensor_copy(dstt[:, q * 512 + j * 128: q * 512 + (j + 1) * 128],
                                                  psTb6[:, q * 128:(q + 1) * 128]),
                                    reads=("psTb6",), writes=(dsn,))
                if not is_q:
                    continue
                bq = kb
                rq, rp = r, (kb + 1) % 3
                for i in range(4):
                    use_ms = (bq in (-1, 0))
                    if use_ms:
                        ti = (bq + 1) * 4 + i
                        self.dma(rec, "A", msk[:, :],
                                 self.d["maskS"].ap()[ti * 128:(ti + 1) * 128, :], writes=("msk",))
                    for hp in range(4):
                        for hh in range(2):
                            h = hp * 2 + hh
                            cti, ro = h // 2, (h % 2) * 64
                            p2 = h % 2
                            attn_, attnf_, attnT_ = attn[p2], attnf[p2], attnT[p2]
                            nmax_, den_ = nmax[p2], den[p2]
                            an, afn, atn = f"attn{p2}", "attnf0", f"attnT{p2}"
                            nmn, dnn = f"nmax{p2}", f"den{p2}"
                            psS, psn_s = psSim[p2], f"psSim{p2}"
                            n1 = (4 - i) * 128
                            self.op(rec, "tensor", lambda e, cti=cti, ro=ro, i=i, n1=n1, rp=rp, psS=psS:
                                    e.matmul(psS[:, 0:n1],
                                             qfm[ro:ro + 64, cti * 512 + i * 128: cti * 512 + (i + 1) * 128],
                                             kfm[rp][ro:ro + 64, cti * 512 + i * 128: cti * 512 + 512],
                                             start=True, stop=True),
                                    reads=("qfm", f"kfm{rp}"), writes=(psn_s,))
                            if i > 0:
                                self.op(rec, "tensor", lambda e, cti=cti, ro=ro, i=i, n1=n1, rq=rq, psS=psS:
                                        e.matmul(psS[:, n1:512],
                                                 qfm[ro:ro + 64, cti * 512 + i * 128: cti * 512 + (i + 1) * 128],
                                                 kfm[rq][ro:ro + 64, cti * 512: cti * 512 + i * 128],
                                                 start=True, stop=True),
                                        reads=("qfm", f"kfm{rq}"), writes=(psn_s,))
                            self.op(rec, "tensor", lambda e, cti=cti, ro=ro, i=i, rq=rq, psS=psS:
                                    e.matmul(psS[:, 512:640],
                                             qfm[ro:ro + 64, cti * 512 + i * 128: cti * 512 + (i + 1) * 128],
                                             kfm[rq][ro:ro + 64, cti * 512 + i * 128: cti * 512 + (i + 1) * 128],
                                             start=True, stop=True),
                                    reads=("qfm", f"kfm{rq}"), writes=(psn_s,))
                            self.op(rec, "vector", lambda e, psS=psS:
                                    e.tensor_add(psS[:, 0:128], psS[:, 0:128],
                                                 self.t["s_maskLo"][:, :]),
                                    reads=(psn_s, "s_maskLo"), writes=(psn_s,))
                            self.op(rec, "vector", lambda e, psS=psS:
                                    e.tensor_add(psS[:, 512:640], psS[:, 512:640],
                                                 self.t["s_maskHi"][:, :]),
                                    reads=(psn_s, "s_maskHi"), writes=(psn_s,))
                            if use_ms:
                                self.op(rec, "vector", lambda e, psS=psS:
                                        e.tensor_add(psS[:, :], psS[:, :], msk[:, :]),
                                        reads=(psn_s, "msk"), writes=(psn_s,))
                            self.op(rec, "vector", lambda e, nmax_=nmax_, psS=psS:
                                    e.tensor_reduce(nmax_[:, :], psS[:, :], AX.X, ALU.max,
                                                    negate=True),
                                    reads=(psn_s,), writes=(nmn,))
                            self.op(rec, "scalar", lambda e, attnf_=attnf_, nmax_=nmax_, den_=den_, psS=psS:
                                    e.activation(attnf_[:, :], psS[:, :], AF.Exp,
                                                 bias=nmax_[:, 0:1], accum_out=den_[:, :]),
                                    reads=(psn_s, nmn), writes=(afn, dnn))
                            self.op(rec, "vector", lambda e, den_=den_:
                                    e.reciprocal(den_[:, :], den_[:, :]),
                                    reads=(dnn,), writes=(dnn,))
                            self.op(rec, "vector", lambda e, attn_=attn_, attnf_=attnf_, den_=den_:
                                    e.tensor_scalar_mul(attn_[:, :], attnf_[:, :], den_[:, 0:1]),
                                    reads=(afn, dnn), writes=(an,))
                            for q in range(5):
                                self.op(rec, "tensor", lambda e, q=q, attn_=attn_:
                                        e.transpose(psTb6[:, q * 128:(q + 1) * 128],
                                                    attn_[:, q * 128:(q + 1) * 128],
                                                    self.t["s_identB"][:, :]),
                                        reads=(an, "s_identB"), writes=("psTb6",))
                            self.op(rec, "vector", lambda e, attnT_=attnT_:
                                    e.tensor_copy(attnT_[:, :], psTb6[:, :]),
                                    reads=("psTb6",), writes=(atn,))
                            for kk in range(5):
                                blk = rp if (i + kk) < 4 else rq
                                jj = (i + kk) % 4
                                self.op(rec, "tensor",
                                        lambda e, kk=kk, blk=blk, jj=jj, h=h, ro=ro, attnT_=attnT_:
                                        e.matmul(psAV[ro:ro + 64, :],
                                                 vtm[blk][:, jj * 512 + h * 64: jj * 512 + (h + 1) * 64],
                                                 attnT_[:, kk * 128:(kk + 1) * 128],
                                                 start=(kk == 0), stop=(kk == 4),
                                                 tile_position=(0, ro)),
                                        reads=(f"vtm{blk}", atn), writes=("psAV",))
                        self.op(rec, "vector", lambda e, hp=hp, i=i:
                                e.tensor_copy(avfm[:, hp * 512 + i * 128: hp * 512 + (i + 1) * 128],
                                              psAV[:, :]),
                                reads=("psAV",), writes=("avfm",))
                dstd, dstn = (self.d["d_a"], "d_a")
                for m in range(NG):
                    for cti in range(4):
                        self.op(rec, "tensor", lambda e, m=m, cti=cti:
                                e.matmul(psQK[m % 2][:, :],
                                         wob[:, cti * DM + m * 128: cti * DM + (m + 1) * 128],
                                         avfm[:, cti * 512:(cti + 1) * 512],
                                         start=(cti == 0), stop=(cti == 3)),
                                reads=("wob", "avfm"), writes=(f"psQK{m % 2}",))
                    if bq >= 0:
                        self.op(rec, "vector", lambda e, m=m, rq=rq:
                                e.tensor_add(aout[m % 2][:, :], psQK[m % 2][:, :],
                                             xsl[rq][m][:, 1:513]),
                                reads=(f"psQK{m % 2}", f"xsl{rq}_{m}"), writes=(f"aout{m % 2}",))
                        self.dma(rec, "S", dstd.ap()[m * 128:(m + 1) * 128,
                                                     1 + bq * WIN: 1 + (bq + 1) * WIN],
                                 aout[m % 2][:, :], reads=(f"aout{m % 2}",), writes=(dstn,))
                    else:
                        self.op(rec, "vector", lambda e, m=m, rq=rq:
                                e.scalar_tensor_tensor(aout[m % 2][:, 0:1],
                                                       psQK[m % 2][:, 511:512],
                                                       hg[:, 0:1],
                                                       xsl[rq][m][:, 512:513], ALU.mult, ALU.add),
                                reads=(f"psQK{m % 2}", "s_hgate", f"xsl{rq}_{m}"),
                                writes=(f"aout{m % 2}",))
                        self.dma(rec, "S", dstd.ap()[m * 128:(m + 1) * 128, 0:1],
                                 aout[m % 2][:, 0:1], reads=(f"aout{m % 2}",), writes=(dstn,),
                                 slow=True)
            self.drain_dmas(rec)
            self.drain_dmas(rec)
        rec.replay(nc)

    def phase_mlp(self, es, l):
        rec = Rec(self)
        nc = self.nc
        NTCB = self.THALF // TCM
        dst, dstn = (self.d["d_m"], "d_m") if l == 0 else (self.d["d_m1"], "d_m1")
        MPW = [128] * 21 + [HID - 21 * 128]
        with contextlib.ExitStack() as les:
            asl = [self.sb(les, f"asl{k}", [128, TCM + 1], F32) for k in range(NG)]
            ash = [self.sb(les, f"ash{k}", [128, 512], BF16) for k in range(NG)]
            sqb = [self.sb(les, f"sqb{k}", [128, 512], BF16) for k in range(NG)]
            lnb = [self.sb(les, f"lnb{k}", [128, TCM], BF16) for k in range(NG)]
            psS1 = self.ps(les, "psS1", [1, 512], F32)
            psS2 = self.ps(les, "psS2", [1, 512], F32)
            psMu = self.ps(les, "psMu", [128, 512], F32)
            psRs = self.ps(les, "psRs", [128, 512], F32)
            psA = self.ps(les, "psA", [128, 512], F32)
            psG = self.ps(les, "psG", [128, 512], F32)
            psW2 = self.ps(les, "psW2", [128, 512], F32)
            murow = self.sb(les, "murow", [1, 512], F32)
            msrow = self.sb(les, "msrow", [1, 512], F32)
            t1r = self.sb(les, "t1r", [1, 512], F32)
            lsc1 = self.sb(les, "lsc1", [128, 512], F32)
            lsc2 = self.sb(les, "lsc2", [128, 512], F32)
            gel = self.sb(les, "gel", [128, 512], F32)
            hb = [self.sb(les, f"hb{m}", [128, TCM], BF16) for m in range(22)]
            w1a = [[self.sb(les, f"w1a{i}_{k}", [128, 128], BF16) for k in range(NG)]
                   for i in range(2)]
            w1g = [[self.sb(les, f"w1g{i}_{k}", [128, 128], BF16) for k in range(NG)]
                   for i in range(2)]
            w2s = [[self.sb(les, f"w2s{i}_{m}", [128, 128], BF16) for m in range(22)]
                   for i in range(2)]
            mout = self.sb(les, "mout", [128, 512], F32)
            onesC = self.t["s_onesC"]
            ones1 = self.t["s_ones1"]
            fg = self.t[f"s_flnG{l}"]
            fb = self.t[f"s_flnB{l}"]
            b1a = self.t[f"s_b1a{l}"]
            b1g = self.t[f"s_b1g{l}"]
            b2 = self.t[f"s_b2{l}"]
            for tcb in range(NTCB):
                a0 = tcb * TCM
                for k in range(NG):
                    self.dma(rec, "S", asl[k][:, :],
                             self.d["d_a"].ap()[k * 128:(k + 1) * 128, a0:a0 + TCM + 1],
                             reads=("d_a",), writes=(f"asl{k}",), grp="asl")
                self.group_fix([f"asl{k}" for k in range(NG)], "asl")
                for tc2 in range(TCM // 512):
                    for k in range(NG):
                        off = (1 if k < 4 else 0) + tc2 * 512
                        self.op(rec, "scalar", lambda e, k=k, off=off:
                                e.copy(ash[k][:, :], asl[k][:, off:off + 512]),
                                reads=(f"asl{k}",), writes=(f"ash{k}",))
                        self.op(rec, "scalar", lambda e, k=k:
                                e.activation(sqb[k][:, :], ash[k][:, :], AF.Square),
                                reads=(f"ash{k}",), writes=(f"sqb{k}",))
                        self.op(rec, "tensor", lambda e, k=k:
                                e.matmul(psS1[:, :], onesC[:, 0:1], ash[k][:, :],
                                         start=(k == 0), stop=(k == NG - 1)),
                                reads=("s_onesC", f"ash{k}"), writes=("psS1",))
                        self.op(rec, "tensor", lambda e, k=k:
                                e.matmul(psS2[:, :], onesC[:, 0:1], sqb[k][:, :],
                                         start=(k == 0), stop=(k == NG - 1)),
                                reads=("s_onesC", f"sqb{k}"), writes=("psS2",))
                    self.op(rec, "vector", lambda e:
                            e.tensor_scalar(murow[:, :], psS1[:, :], 1.0 / DM, None, ALU.mult),
                            reads=("psS1",), writes=("murow",))
                    self.op(rec, "vector", lambda e:
                            e.tensor_scalar(msrow[:, :], psS2[:, :], 1.0 / DM, None, ALU.mult),
                            reads=("psS2",), writes=("msrow",))
                    self.op(rec, "vector", lambda e:
                            e.tensor_mul(t1r[:, :], murow[:, :], murow[:, :]),
                            reads=("murow",), writes=("t1r",))
                    self.op(rec, "vector", lambda e:
                            e.tensor_sub(msrow[:, :], msrow[:, :], t1r[:, :]),
                            reads=("msrow", "t1r"), writes=("msrow",))
                    self.op(rec, "scalar", lambda e:
                            e.activation(msrow[:, :], msrow[:, :], AF.Ln,
                                         bias=self.t["s_eps"][0:1, 0:1]),
                            reads=("msrow", "s_eps"), writes=("msrow",))
                    self.op(rec, "scalar", lambda e:
                            e.activation(msrow[:, :], msrow[:, :], AF.Exp,
                                         scale=self.t["s_negh"][0:1, 0:1]),
                            reads=("msrow", "s_negh"), writes=("msrow",))
                    self.op(rec, "tensor", lambda e:
                            e.matmul(psMu[:, :], ones1[0:1, :], murow[0:1, :],
                                     start=True, stop=True),
                            reads=("s_ones1", "murow"), writes=("psMu",))
                    self.op(rec, "tensor", lambda e:
                            e.matmul(psRs[:, :], ones1[0:1, :], msrow[0:1, :],
                                     start=True, stop=True),
                            reads=("s_ones1", "msrow"), writes=("psRs",))
                    for k in range(NG):
                        self.op(rec, "vector", lambda e, k=k:
                                e.tensor_sub(lsc1[:, :], ash[k][:, :], psMu[:, :]),
                                reads=(f"ash{k}", "psMu"), writes=("lsc1",))
                        self.op(rec, "vector", lambda e, k=k:
                                e.scalar_tensor_tensor(lsc2[:, :], lsc1[:, :], fg[:, k:k + 1],
                                                       psRs[:, :], ALU.mult, ALU.mult),
                                reads=("lsc1", f"s_flnG{l}", "psRs"), writes=("lsc2",))
                        self.op(rec, "vector", lambda e, k=k, tc2=tc2:
                                e.tensor_scalar_add(lnb[k][:, tc2 * 512:(tc2 + 1) * 512],
                                                    lsc2[:, :], fb[:, k:k + 1]),
                                reads=("lsc2", f"s_flnB{l}"), writes=(f"lnb{k}",))
                for mp in range(22):
                    wa = MPW[mp]
                    ib = mp % 2
                    for k in range(NG):
                        self.dma(rec, "S", w1a[ib][k][0:128, 0:wa],
                                 self.d[f"d_w1b{l}"].ap()[k * 128:(k + 1) * 128,
                                                          mp * 128:mp * 128 + wa],
                                 reads=(f"d_w1b{l}",), writes=(f"w1a{ib}_{k}",))
                        self.dma(rec, "S", w1g[ib][k][0:128, 0:wa],
                                 self.d[f"d_w1b{l}"].ap()[k * 128:(k + 1) * 128,
                                                          HID + mp * 128:HID + mp * 128 + wa],
                                 reads=(f"d_w1b{l}",), writes=(f"w1g{ib}_{k}",))
                    self.group_fix([f"w1a{ib}_{k}" for k in range(NG)], f"w1a{ib}")
                    self.group_fix([f"w1g{ib}_{k}" for k in range(NG)], f"w1g{ib}")
                    for mc in range(TCM // 512):
                        for k in range(NG):
                            self.op(rec, "tensor", lambda e, k=k, ib=ib, wa=wa, mc=mc:
                                    e.matmul(psA[0:wa, :], w1a[ib][k][:, 0:wa],
                                             lnb[k][:, mc * 512:(mc + 1) * 512],
                                             start=(k == 0), stop=(k == NG - 1)),
                                    reads=(f"w1a{ib}_{k}", f"lnb{k}"), writes=("psA",))
                            self.op(rec, "tensor", lambda e, k=k, ib=ib, wa=wa, mc=mc:
                                    e.matmul(psG[0:wa, :], w1g[ib][k][:, 0:wa],
                                             lnb[k][:, mc * 512:(mc + 1) * 512],
                                             start=(k == 0), stop=(k == NG - 1)),
                                    reads=(f"w1g{ib}_{k}", f"lnb{k}"), writes=("psG",))
                        self.op(rec, "scalar", lambda e, wa=wa, mp=mp:
                                e.activation(gel[0:wa, :], psG[0:wa, :], AF.Gelu,
                                             bias=b1g[0:wa, mp:mp + 1]),
                                reads=("psG", f"s_b1g{l}"), writes=("gel",))
                        self.op(rec, "vector", lambda e, wa=wa, mp=mp, mc=mc:
                                e.scalar_tensor_tensor(hb[mp][0:wa, mc * 512:(mc + 1) * 512],
                                                       psA[0:wa, :], b1a[0:wa, mp:mp + 1],
                                                       gel[0:wa, :], ALU.add, ALU.mult),
                                reads=("psA", f"s_b1a{l}", "gel"), writes=(f"hb{mp}",))
                for m2 in range(NG):
                    ib = m2 % 2
                    for mp in range(22):
                        self.dma(rec, "S", w2s[ib][mp][0:MPW[mp], :],
                                 self.d[f"d_w2b{l}"].ap()[mp * 128:mp * 128 + MPW[mp],
                                                          m2 * 128:(m2 + 1) * 128],
                                 reads=(f"d_w2b{l}",), writes=(f"w2s{ib}_{mp}",))
                    self.group_fix([f"w2s{ib}_{mp}" for mp in range(22)], f"w2s{ib}")
                    for mc in range(TCM // 512):
                        for mp in range(22):
                            self.op(rec, "tensor", lambda e, mp=mp, ib=ib, mc=mc:
                                    e.matmul(psW2[:, :], w2s[ib][mp][0:MPW[mp], :],
                                             hb[mp][0:MPW[mp], mc * 512:(mc + 1) * 512],
                                             start=(mp == 0), stop=(mp == 21)),
                                    reads=(f"w2s{ib}_{mp}", f"hb{mp}"), writes=("psW2",))
                        self.op(rec, "vector", lambda e, m2=m2, mc=mc:
                                e.scalar_tensor_tensor(mout[:, :], psW2[:, :], b2[:, m2:m2 + 1],
                                                       asl[m2][:, 1 + mc * 512:1 + (mc + 1) * 512],
                                                       ALU.add, ALU.add),
                                reads=("psW2", f"s_b2{l}", f"asl{m2}"), writes=("mout",))
                        self.dma(rec, "S", dst.ap()[m2 * 128:(m2 + 1) * 128,
                                                    1 + a0 + mc * 512:1 + a0 + (mc + 1) * 512],
                                 mout[:, :], reads=("mout",), writes=(dstn,))
            self.drain_dmas(rec)
            self.drain_dmas(rec)
        rec.replay(nc)

    def phase_ag(self, es):
        rec = Rec(self)
        with contextlib.ExitStack() as les:
            gh = self.sb(les, "gh", [128, 1025], F32)
            self.dma(rec, "G", self.d["d_m0hi"].ap()[:, :],
                     self.d["d_m"].ap()[:, 1 + (self.THALF - 1025):1 + self.THALF],
                     reads=("d_m",), writes=("d_m0hi",))
            self.op(rec, "gpsimd", lambda e: e.collective_compute(
                "AllGather", ALU.bypass, replica_groups=PAIRS,
                ins=[self.d["d_m0hi"].ap().opt()], outs=[self.d["d_m0ho"].ap().opt()]),
                reads=("d_m0hi",), writes=("d_m0ho",))
            hgt = self.t["s_hgate"]
            for k in range(NG):
                self.dma(rec, "S", gh[:, :],
                         self.d["d_m0ho"].ap()[k * 128:(k + 1) * 128, :],
                         reads=("d_m0ho",), writes=("gh",))
                self.op(rec, "vector", lambda e:
                        e.tensor_scalar_mul(gh[:, :], gh[:, :], hgt[:, 0:1]),
                        reads=("gh", "s_hgate"), writes=("gh",))
                self.dma(rec, "S", self.d["d_mh"].ap()[k * 128:(k + 1) * 128, :],
                         gh[:, :], reads=("gh",), writes=("d_mh",))
                self.dma(rec, "S", self.d["d_m"].ap()[k * 128:(k + 1) * 128, 0:1],
                         gh[:, 1024:1025], reads=("gh",), writes=("d_m",), slow=True)
            self.drain_dmas(rec)
            self.drain_dmas(rec)
        rec.replay(self.nc)

    def phase_merge(self, es):
        rec = Rec(self)
        NMC = self.THALF // 512
        with contextlib.ExitStack() as les:
            pssl = [self.sb(les, f"pssl{k}", [128, 512], F32) for k in range(NG)]
            xnsl = [self.sb(les, f"xnsl{k}", [128, 512], F32) for k in range(NG)]
            m1sl = [self.sb(les, f"m1sl{k}", [128, 512], F32) for k in range(NG)]
            sgl = [self.sb(les, f"sgl{k}", [128, 512], F32) for k in range(NG)]
            st_ = self.sb(les, "st_", [128, 512], F32)
            gt_ = self.sb(les, "gt_", [128, 512], F32)
            psTf = [self.ps(les, f"psTm{i}", [128, 512], F32) for i in range(2)]
            xtm = self.sb(les, "xtm", [128, DM], F32)
            otm = self.sb(les, "otm", [128, DM], F32)
            rsc = self.t["s_rsc"]
            for mc in range(NMC):
                c0 = mc * 512
                for k in range(NG):
                    self.dma(rec, "S", pssl[k][:, :],
                             self.d["d_ps"].ap()[k * 128:(k + 1) * 128, c0:c0 + 512],
                             reads=("d_ps",), writes=(f"pssl{k}",), grp="pssl")
                    self.dma(rec, "A", xnsl[k][:, :],
                             self.d["d_xn"].ap()[k * 128:(k + 1) * 128, HALO + c0:HALO + c0 + 512],
                             reads=("d_xn",), writes=(f"xnsl{k}",), grp="xnsl")
                    self.dma(rec, "A", m1sl[k][:, :],
                             self.d["d_m1"].ap()[k * 128:(k + 1) * 128, 1 + c0:1 + c0 + 512],
                             reads=("d_m1",), writes=(f"m1sl{k}",), grp="m1sl")
                self.group_fix([f"pssl{k}" for k in range(NG)], "pssl")
                self.group_fix([f"xnsl{k}" for k in range(NG)], "xnsl")
                self.group_fix([f"m1sl{k}" for k in range(NG)], "m1sl")
                for k in range(NG):
                    self.op(rec, "vector", lambda e, k=k:
                            e.tensor_add(st_[:, :], pssl[k][:, :], xnsl[k][:, :]),
                            reads=(f"pssl{k}", f"xnsl{k}"), writes=("st_",))
                    self.op(rec, "scalar", lambda e, k=k:
                            e.activation(gt_[:, :], m1sl[k][:, :], AF.Silu),
                            reads=(f"m1sl{k}",), writes=("gt_",))
                    self.op(rec, "vector", lambda e, k=k:
                            e.tensor_mul(sgl[k][:, :], st_[:, :], gt_[:, :]),
                            reads=("st_", "gt_"), writes=(f"sgl{k}",))
                for tt in range(4):
                    self.dma(rec, "S", xtm[:, :],
                             self.d["xOwn"].ap()[c0 + tt * 128:c0 + tt * 128 + 128, :],
                             writes=("xtm",))
                    for half in range(2):
                        pt = psTf[half]
                        ptn = f"psTm{half}"
                        for q in range(4):
                            k = half * 4 + q
                            self.op(rec, "tensor", lambda e, pt=pt, q=q, k=k, tt=tt:
                                    e.transpose(pt[:, q * 128:(q + 1) * 128],
                                                sgl[k][:, tt * 128:(tt + 1) * 128],
                                                self.t["s_identF"][:, :]),
                                    reads=(f"sgl{k}", "s_identF"), writes=(ptn,))
                        self.op(rec, "vector", lambda e, pt=pt, half=half:
                                e.tensor_add(otm[:, half * 512:(half + 1) * 512],
                                             xtm[:, half * 512:(half + 1) * 512], pt[:, :]),
                                reads=("xtm", ptn), writes=("otm",))
                    self.dma(rec, "S", self.d["OUT"].ap()[c0 + tt * 128:c0 + tt * 128 + 128, :],
                             otm[:, :], reads=("otm",), writes=("OUT",))
            rec.wait_only("sync", [(self.ch["dS"], self.ch["dS"].n)])
            self.drain_dmas(rec)
            self.drain_dmas(rec)
        rec.replay(self.nc)


def build_program(tseq=TSEQ, phases=None):
    nc = bass.Bass("TRN2", target_bir_lowering=False)
    es = contextlib.ExitStack()
    with es:
        mk = MK(nc, es, tseq=tseq)
        mk._init_track()
        mk.declare()
        steps = [
            ("setup", lambda: mk.setup(es)),
            ("wprep", lambda: mk.wprep(es)),
            ("a1", lambda: mk.phase_a1(es)),
            ("a2", lambda: mk.phase_a2(es)),
            ("bprep", lambda: mk.phase_bprep(es)),
            ("attn0", lambda: mk.phase_attn(es, 0)),
            ("mlp0", lambda: mk.phase_mlp(es, 0)),
            ("ag", lambda: mk.phase_ag(es)),
            ("attn1", lambda: mk.phase_attn(es, 1)),
            ("mlp1", lambda: mk.phase_mlp(es, 1)),
            ("merge", lambda: mk.phase_merge(es)),
        ]
        for name, fn in steps:
            if phases is None or name in phases:
                fn()
    return nc


_FULL = {
    "L0_attn_seq", "L0_attn_ln_g", "L0_attn_ln_b", "L0_attn_in_proj_w",
    "L0_attn_conv_w", "L0_attn_conv_b", "L0_attn_x_proj_w", "L0_attn_dt_proj_w",
    "L0_attn_dt_proj_b", "L0_attn_A_log", "L0_attn_D", "L0_attn_out_proj_w",
    "L0_attn_res_scale", "L0_ffn_wqkv", "L0_ffn_q_scale", "L0_ffn_k_scale",
    "L0_ffn_wo", "L0_ffn_ln_g", "L0_ffn_ln_b", "L0_ffn_w1", "L0_ffn_b1",
    "L0_ffn_w2", "L0_ffn_b2",
}


def kernel(**inputs):
    I = {k: np.asarray(v) for k, v in inputs.items()}
    nc = build_program()
    in_maps = [make_core_inputs(c, I) for c in range(8)]
    res = run_bass_kernel_spmd(nc, in_maps, core_ids=list(range(8)))
    out = np.empty((BSZ, TSEQ, DM), np.float32)
    for c in range(8):
        b, h = c // 2, c % 2
        out[b, h * TH:(h + 1) * TH, :] = res.results[c]["OUT"]
    return out
